# version 36
# baseline (speedup 1.0000x reference)
import numpy as np
import ml_dtypes
import concourse.bass as bass
import concourse.mybir as mybir
from concourse.bass_utils import run_bass_kernel_spmd

F32 = mybir.dt.float32
BF16 = mybir.dt.bfloat16
AF = mybir.ActivationFunctionType
ALU = mybir.AluOpType
AX = mybir.AxisListType

T = 4096
D = 1024
E = 2048
S = 16
R = 64
TC = 512
NCH = T // TC
NG = E // 128
ND = D // 128
NT = TC // 128
EPS = 1e-5
BF = ml_dtypes.bfloat16


class Plan:

    ENGINES = ("sync", "gpsimd", "tensor", "vector", "scalar")

    def __init__(self):
        self.ops = []
        self.dsem_counts = {}

    def add(self, engine, fn, after=(), dma=False, dsem=None):
        idx = len(self.ops)
        for j in after:
            assert 0 <= j < idx, (j, idx)
        assert (dsem is not None) == bool(dma)
        self.ops.append(dict(engine=engine, fn=fn, after=list(after),
                             dma=dma, dsem=dsem, marked=False, val=None))
        return idx

    def finalize(self):
        for op in self.ops:
            if op["dma"]:
                op["marked"] = True
            for j in op["after"]:
                pj = self.ops[j]
                if pj["engine"] != op["engine"] or pj["dma"]:
                    pj["marked"] = True
        cnt = {e: 0 for e in self.ENGINES}
        for op in self.ops:
            if not op["marked"]:
                continue
            if op["dma"]:
                c = self.dsem_counts.get(op["dsem"], 0) + 16
                self.dsem_counts[op["dsem"]] = c
                op["val"] = c
            else:
                cnt[op["engine"]] += 1
                op["val"] = cnt[op["engine"]]
        self.final_counts = cnt

    def emit_engine(self, ename, eng, sems, dsems):
        for op in self.ops:
            if op["engine"] != ename:
                continue
            waits = {}
            for j in op["after"]:
                pj = self.ops[j]
                if pj["dma"]:
                    k = ("d", pj["dsem"])
                elif pj["engine"] == ename:
                    continue
                else:
                    k = ("e", pj["engine"])
                waits[k] = max(waits.get(k, 0), pj["val"])
            for (kind, s), v in waits.items():
                eng.wait_ge(dsems[s] if kind == "d" else sems[s], v)
            inst = op["fn"](eng)
            if op["marked"]:
                assert inst is not None, "marked op returned no instruction"
                if op["dma"]:
                    inst.then_inc(dsems[op["dsem"]], 16)
                else:
                    inst.then_inc(sems[ename], 1)


class _StopBuild(Exception):
    pass


def build_nc(nch=NCH, dbg=None):
    nc = bass.Bass("TRN2", target_bir_lowering=False)

    di = lambda n, sh, dt=F32: nc.dram_tensor(n, sh, dt, kind="ExternalInput").ap()
    xb = di("xb", [T, D])
    w_inT = di("w_inT", [D, 2 * E])
    w_xT = di("w_xT", [E, 96])
    w_dtT = di("w_dtT", [R, E])
    w_outT = di("w_outT", [E, D])
    convw_l = di("convw_l", [128, NG * 4])
    convb_l = di("convb_l", [128, NG])
    bdt_l = di("bdt_l", [128, NG])
    d_l = di("d_l", [128, NG])
    alog_l = di("alog_l", [128, NG * S])
    wfr = di("wfr", [D, D])
    wfi = di("wfi", [D, D])
    bias_l = di("bias_l", [128, ND])
    fr_c = di("fr_c", [D, D], BF16)
    fi_c = di("fi_c", [D, D], BF16)
    fineg_c = di("fineg_c", [D, D], BF16)
    gr_c = di("gr_c", [D, D], BF16)
    gineg_c = di("gineg_c", [D, D], BF16)
    idb_c = di("idb_c", [128, 128], BF16)

    yb = nc.dram_tensor("yb", [T, D], F32, kind="ExternalOutput").ap()
    bc_stage = nc.dram_tensor("bc_stage", [32, TC], BF16).ap()
    w_x_dram = nc.dram_tensor("w_x_dram", [NG, 128, ND * 128], BF16).ap()
    w_z_dram = nc.dram_tensor("w_z_dram", [NG, 128, ND * 128], BF16).ap()
    w_o_dram = nc.dram_tensor("w_o_dram", [ND, 128, NG * 128], BF16).ap()

    plan = Plan()
    P = plan.add

    import contextlib
    with contextlib.ExitStack() as _es:
        _e = _es.enter_context
        arena_u8 = _e(nc.sbuf_tensor("arena", [128, 80 * 1024],
                                     mybir.dt.uint8))
        m_bf = _e(nc.sbuf_tensor("m_bf", [128, ND * D], BF16))
        w_xT_bf = _e(nc.sbuf_tensor("w_xT_bf", [128, NG * 96], BF16))
        w_dtT_bf = _e(nc.sbuf_tensor("w_dtT_bf", [64, E], BF16))
        xw_ring = _e(nc.sbuf_tensor("xw_ring", [128, 3 * ND * 128], BF16))
        zw_ring = _e(nc.sbuf_tensor("zw_ring", [128, 3 * ND * 128], BF16))
        wo_ring = _e(nc.sbuf_tensor("wo_ring", [128, 2 * NG * 128], BF16))
        a_sb = _e(nc.sbuf_tensor("a_sb", [128, NG * S], F32))
        convw_sb = _e(nc.sbuf_tensor("convw_sb", [128, NG * 4], F32))
        convb_sb = _e(nc.sbuf_tensor("convb_sb", [128, NG], F32))
        bdt_sb = _e(nc.sbuf_tensor("bdt_sb", [128, NG], F32))
        d_sb = _e(nc.sbuf_tensor("d_sb", [128, NG], F32))
        bias_sb = _e(nc.sbuf_tensor("bias_sb", [128, ND], F32))
        carry = _e(nc.sbuf_tensor("carry", [128, NG * S], F32))
        id_b = _e(nc.sbuf_tensor("id_b", [128, 128], BF16))
        x_tile2 = _e(nc.sbuf_tensor("x_tile2", [128, 2 * D], F32))
        lnx_t2 = _e(nc.sbuf_tensor("lnx_t2", [128, 2 * D], BF16))
        lnx_d = _e(nc.sbuf_tensor("lnx_d", [128, ND * TC], BF16))
        xc_bf = _e(nc.sbuf_tensor("xc_bf", [128, NG * TC], BF16))
        sz_b = _e(nc.sbuf_tensor("sz_b", [128, 4 * TC], BF16))
        dt_b = _e(nc.sbuf_tensor("dt_b", [128, 2 * TC], F32))
        da_b = _e(nc.sbuf_tensor("da_b", [128, 4 * TC], F32))
        conv_acc = _e(nc.sbuf_tensor("conv_acc", [128, TC], F32))
        scr_f = _e(nc.sbuf_tensor("scr_f", [128, 2 * D], F32))
        sq_dump = _e(nc.sbuf_tensor("sq_dump", [128, D], F32))
        xi_ext = _e(nc.sbuf_tensor("xi_ext", [128, TC + 4], F32))
        tails = _e(nc.sbuf_tensor("tails", [128, NG * 4], F32))
        dtr_bf = _e(nc.sbuf_tensor("dtr_bf", [64, TC], BF16))
        bc_bf = _e(nc.sbuf_tensor("bc_bf", [32, TC], BF16))
        dtx_bf = _e(nc.sbuf_tensor("dtx_bf", [128, TC], BF16))
        y_f = _e(nc.sbuf_tensor("y_f", [128, TC], F32))
        yz_f = _e(nc.sbuf_tensor("yz_f", [128, TC], F32))
        stats = _e(nc.sbuf_tensor("stats", [128, 128], F32))
        pb0 = _e(nc.psum_tensor("pb0", [128, 512], F32))
        pb1 = _e(nc.psum_tensor("pb1", [128, 512], F32))
        pb2 = _e(nc.psum_tensor("pb2", [128, 512], F32))
        pb3 = _e(nc.psum_tensor("pb3", [128, 512], F32))
        pb4 = _e(nc.psum_tensor("pb4", [128, 512], F32))
        pb5 = _e(nc.psum_tensor("pb5", [128, 1024], BF16))
        pb6 = _e(nc.psum_tensor("pb6", [128, 512], F32))
        pb7 = _e(nc.psum_tensor("pb7", [128, 1024], BF16))
        s_sync = _e(nc.semaphore("s_sync"))
        s_gps = _e(nc.semaphore("s_gps"))
        s_pe = _e(nc.semaphore("s_pe"))
        s_dve = _e(nc.semaphore("s_dve"))
        s_act = _e(nc.semaphore("s_act"))
        DSEM_NAMES = (["c0", "la0", "la1", "la2", "la3",
                       "lb0", "lb1", "lb2", "lb3", "cs0", "cs1", "cs2",
                       "cb0", "cb1", "xw0", "xw1", "xw2", "zw0", "zw1",
                       "zw2", "wo0", "wo1", "xt", "bco", "bcb", "bcc",
                       "st0", "st1", "st2", "st3", "dbg"] +
                      ["w%d%d" % (i, q) for i in range(3) for q in range(2)] +
                      ["q%d%d" % (i, q) for i in range(3) for q in range(4)] +
                      ["A%d%d" % (i, q) for i in range(4) for q in range(2)] +
                      ["B%d%d" % (i, q) for i in range(4) for q in range(2)] +
                      ["bcb1"])
        dsems = {n: _e(nc.semaphore("sd_" + n)) for n in DSEM_NAMES}
        block = _e(nc.Block())
        sems = dict(sync=s_sync, gpsimd=s_gps, tensor=s_pe,
                    vector=s_dve, scalar=s_act)

        def dump(name, ap, after):
            if dbg != name:
                return
            dh = nc.dram_tensor("dbg_" + name, list(ap.shape), ap.dtype,
                                kind="ExternalOutput").ap()
            dd = P("sync", lambda e, ap=ap, dh=dh: e.dma_start(
                out=dh, in_=ap), dma=True, dsem="dbg", after=after)
            P("sync", lambda e: e.nop(), after=[dd])
            raise _StopBuild()
        arena = arena_u8[:]

        def av(off_kb, n_elems, dt=F32):
            bpe = 4 if dt == F32 else (2 if dt == BF16 else 1)
            return arena[:, off_kb * 1024: off_kb * 1024 + n_elems * bpe].bitcast(dt)

        B_rep = lambda: av(0, S * TC, BF16)
        C_rep = lambda: av(16, S * TC, BF16)
        u_v = lambda: av(32, S * TC, BF16)
        prb = lambda: av(48, S * TC, BF16)
        mo_d = lambda: av(0, ND * TC, BF16)
        mo_t = lambda: av(16, NT * D, F32)
        x2_v = lambda: av(64, NT * D, F32)

        try:
            def ld(dst, src, dsem, after=()):
                return P("sync", lambda e, d=dst, s=src: e.dma_start(out=d, in_=s),
                         dma=True, dsem=dsem, after=after)

            ld(convw_sb[:], convw_l[:], "c0")
            ld(convb_sb[:], convb_l[:], "c0")
            ld(bdt_sb[:], bdt_l[:], "c0")
            ld(d_sb[:], d_l[:], "c0")
            ld(a_sb[:], alog_l[:], "c0")
            ld(bias_sb[:], bias_l[:], "c0")
            consts_done = ld(id_b[:], idb_c[:], "c0")

            a_exp = P("scalar", lambda e: e.activation(a_sb[:], a_sb[:], AF.Exp),
                      after=[consts_done])
            a_fix = P("vector", lambda e: e.tensor_scalar_mul(
                a_sb[:], a_sb[:], -1.0), after=[a_exp])
            z_carry = P("gpsimd", lambda e: e.memset(carry[:], 0.0))
            c_eps = P("gpsimd", lambda e: e.memset(stats[:, 120:121], EPS))
            c_half = P("gpsimd", lambda e: e.memset(stats[:, 121:122], -0.5))
            z_tails = P("gpsimd", lambda e: e.memset(tails[:], 0.0))

            wr_bf = lambda: av(0, ND * D, BF16)
            wi_bf = lambda: av(16, ND * D, BF16)
            htr = lambda: av(32, ND * D, BF16)
            hti = lambda: av(48, ND * D, BF16)
            ca_v = lambda p: av(64 + 2 * (p % 4), D, BF16)
            cb_v = lambda p: av(72 + 2 * (p % 4), D, BF16)

            wdeps = []
            j = 0
            for i in range(ND):
                for which, dst in ((wfr, wr_bf), (wfi, wi_bf)):
                    tmp = av(32 + 8 * (j % 3), D, F32)
                    war = [wdeps[j - 3]] if j >= 3 else []
                    ls = [P(["sync", "gpsimd"][q],
                            lambda e, i=i, t=tmp, w=which, q=q: e.dma_start(
                        out=t[:, q * 512:(q + 1) * 512],
                        in_=w[i * 128:(i + 1) * 128, q * 512:(q + 1) * 512]),
                        dma=True, dsem="w%d%d" % (j % 3, q), after=war)
                        for q in range(2)]
                    c1 = P("vector", lambda e, i=i, t=tmp, dst=dst:
                           e.tensor_copy(
                        dst()[:, i * D:(i + 1) * D], t[:]), after=ls)
                    wdeps.append(c1)
                    j += 1

            def cplx_pass(wA_fn, constA, wB_fn, constB, out_sb, out_stride,
                          extra_after):
                res = []
                prev_evacs = []
                for mp in range(ND // 2):
                    kdeps = []
                    for k in range(ND):
                        lwar = (kdeps[-3:-2] if k >= 3 else
                                (([] if extra_after and
                                  extra_after[0] == wdeps[-1]
                                  else extra_after)
                                 if mp == 0 else prev_evacs[-1:]))
                        la_ = [P(["sync", "gpsimd"][q],
                                 lambda e, k=k, q=q: e.dma_start(
                            out=ca_v(k % 4)[:, q * 512:(q + 1) * 512],
                            in_=constA[k * 128:(k + 1) * 128,
                                       q * 512:(q + 1) * 512]),
                            dma=True, dsem="A%d%d" % (k % 4, q), after=lwar)
                            for q in range(2)]
                        lb_ = [P(["scalar", "gpsimd"][q],
                                 lambda e, k=k, q=q: e.dma_start(
                            out=cb_v(k % 4)[:, q * 512:(q + 1) * 512],
                            in_=constB[k * 128:(k + 1) * 128,
                                       q * 512:(q + 1) * 512]),
                            dma=True, dsem="B%d%d" % (k % 4, q), after=lwar)
                            for q in range(2)]
                        war = (prev_evacs[4 * (mp - 1):4 * (mp - 1) + 4]
                               if mp >= 1 and k == 0 else [])

                        def fmm(e, mp=mp, k=k):
                            r = None
                            for m in (2 * mp, 2 * mp + 1):
                                pA = [pb0, pb1][m % 2]
                                pB = [pb2, pb3][m % 2]
                                for nt, pp in ((0, pA), (1, pB)):
                                    e.matmul(pp[:],
                                             wA_fn()[:, k * D + m * 128:
                                                     k * D + m * 128 + 128],
                                             ca_v(k % 4)[:, nt * 512: nt * 512 + 512],
                                             start=(k == 0), stop=False)
                                    r = e.matmul(pp[:],
                                                 wB_fn()[:, k * D + m * 128:
                                                         k * D + m * 128 + 128],
                                                 cb_v(k % 4)[:, nt * 512: nt * 512 + 512],
                                                 start=False, stop=(k == ND - 1))
                            return r
                        wk = (wdeps[2 * k:2 * k + 2]
                              if (extra_after and extra_after[0] == wdeps[-1])
                              else [])
                        mmk = P("tensor", fmm, after=la_ + lb_ + war + wk)
                        kdeps.append(mmk)
                    for m in (2 * mp, 2 * mp + 1):
                        pA = [pb0, pb1][m % 2]
                        pB = [pb2, pb3][m % 2]
                        for nt, pp in ((0, pA), (1, pB)):
                            ev = P("vector", lambda e, m=m, nt=nt, pp=pp:
                                   e.tensor_copy(
                                out_sb()[:, m * out_stride + nt * 512:
                                         m * out_stride + nt * 512 + 512],
                                pp[:]), after=[kdeps[-1]])
                            prev_evacs.append(ev)
                    res.append(prev_evacs[-1])
                return res

            htr_done = cplx_pass(wr_bf, fr_c, wi_bf, fineg_c, htr, D,
                                 [wdeps[-1]])
            hti_done = cplx_pass(wr_bf, fi_c, wi_bf, fr_c, hti, D,
                                 [htr_done[-1]])
            m_done = cplx_pass(htr, gr_c, hti, gineg_c,
                               lambda: m_bf[:], D, [hti_done[-1]])
            dump("m", m_bf[:], [m_done[-1]])

            cast_sl = lambda p: av(8 * (p % 3), 2048, F32)
            cast_bf = lambda p: av(24 + 4 * (p % 2), 2048, BF16)

            def cast_stream(src_rows_fn, n_k, n_half, out_dmas_fn, after):
                deps = list(after)
                casts = []
                dd_groups = []
                j = 0
                for k in range(n_k):
                    for h in range(n_half):
                        sl = cast_sl(j)
                        war = [casts[j - 3]] if j >= 3 else deps
                        ncols = src_rows_fn(k, h).shape[1]
                        qw = ncols // 4
                        lds = [P(["sync", "gpsimd", "scalar", "gpsimd"][q],
                                 lambda e, k=k, h=h, sl=sl, qw=qw, q=q:
                                 e.dma_start(
                                     out=sl[:, q * qw:(q + 1) * qw],
                                     in_=src_rows_fn(k, h)[:, q * qw:
                                                           (q + 1) * qw]),
                                 dma=True, dsem="q%d%d" % (j % 3, q),
                                 after=war) for q in range(4)]
                        bfw = dd_groups[j - 2] if j >= 2 else deps
                        c = P("vector", lambda e, j=j, sl=sl, ncols=ncols:
                              e.tensor_copy(cast_bf(j)[:, 0:ncols],
                                            sl[:, 0:ncols]),
                              after=lds + bfw)
                        casts.append(c)
                        dds = []
                        for od_out, od_cols in out_dmas_fn(k, h):
                            ng = od_out.shape[0]
                            dds.append(P(
                                "sync",
                                lambda e, o=od_out, cc=od_cols, j=j, ng=ng:
                                e.dma_start(
                                    out=o.rearrange("g p c -> p g c"),
                                    in_=cast_bf(j)[:, cc[0]:cc[1]].rearrange(
                                        "p (g c) -> p g c", g=ng)),
                                dma=True, dsem="cb%d" % (j % 2), after=[c]))
                        dd_groups.append(dds)
                        j += 1
                return dd_groups[-1] + dd_groups[-2]

            def win_src(k, h):
                return w_inT[k * 128:(k + 1) * 128, h * E:(h + 1) * E]

            def win_out(k, h):
                tgt = w_x_dram if h == 0 else w_z_dram
                return [(tgt[jj0:jj0 + 8, :, k * 128:(k + 1) * 128],
                         (jj0 * 128, (jj0 + 8) * 128)) for jj0 in (0, 8)]

            win_done = cast_stream(win_src, ND, 2, win_out, [hti_done[-1]])

            def wo_src(k, h):
                return w_outT[k * 128:(k + 1) * 128, 0:D]

            def wo_out(k, h):
                return [(w_o_dram[:, :, k * 128:(k + 1) * 128],
                         (0, ND * 128))]

            wo_done = cast_stream(wo_src, NG, 1, wo_out, win_done)

            l_wx = P("sync", lambda e: e.dma_start(
                out=cast_sl(0)[:, 0:NG * 96].rearrange(
                    "p (g c) -> p g c", g=NG),
                in_=w_xT.rearrange("(g p) c -> p g c", p=128)),
                dma=True, dsem="cs0", after=wo_done)
            c_wx = P("vector", lambda e: e.tensor_copy(
                w_xT_bf[:], cast_sl(0)[:, 0:NG * 96]), after=[l_wx])
            l_wdt = P("sync", lambda e: e.dma_start(
                out=cast_sl(1)[0:64, :], in_=w_dtT[:]),
                dma=True, dsem="cs1", after=wo_done)
            c_wdt = P("vector", lambda e: e.tensor_copy(
                w_dtT_bf[:], cast_sl(1)[0:64, :]), after=[l_wdt, c_wx])

            phase0_done = ([m_done[-1], a_fix, z_carry, z_tails, consts_done,
                            c_wx, c_wdt, c_eps, c_half] + win_done + wo_done)

            ln_ctr = [0]

            def ln_stats(src_ap, after):
                c = ln_ctr[0] % 8
                ln_ctr[0] += 1
                col = lambda i: stats[:, 8 * c + i: 8 * c + i + 1]
                mraw, sraw, m_col, em_col, lv_col, r_col, mr_col, nmr_col = (
                    col(i) for i in range(8))
                i1 = P("scalar", lambda e: e.activation(sq_dump[:], src_ap,
                                                        AF.Identity,
                                                        accum_out=mraw),
                       after=after)
                i2 = P("scalar", lambda e: e.activation(sq_dump[:], src_ap,
                                                        AF.Square,
                                                        accum_out=sraw),
                       after=after + [i1])
                c0 = P("scalar", lambda e: e.activation(m_col, mraw,
                                                        AF.Identity,
                                                        scale=1.0 / D),
                       after=[i1])
                c1 = P("scalar", lambda e: e.activation(em_col, m_col,
                                                        AF.Square),
                       after=[c0])
                c2 = P("scalar", lambda e: e.activation(
                    em_col, em_col, AF.Identity, scale=-1.0,
                    bias=stats[:, 120:121]), after=[c1, c_eps])
                c3 = P("scalar", lambda e: e.activation(
                    lv_col, sraw, AF.Ln, scale=1.0 / D, bias=em_col),
                    after=[c2, i2])
                c4 = P("scalar", lambda e: e.activation(
                    r_col, lv_col, AF.Exp, scale=stats[:, 121:122]),
                    after=[c3, c_half])
                return m_col, r_col, [c4]

            prev_chunk_tail = [consts_done, c_eps, c_half, z_tails, z_carry,
                               a_fix]
            prev_apply = []
            store_ops = []
            prev_dA = {}
            prev_crep = []
            tr_evs = []
            xw_loads, xw_mms = [], []
            zw_loads, zw_mms = [], []
            wo_loads, wo_mms = [], []

            def transpose_block(src_fn, dst_fn, after_src, extra_first=()):
                evs = []
                for jj in range(ND):
                    pbt = [pb5, pb7][jj % 2]
                    sl = (jj // 2) * 128
                    war = tr_evs[-2:-1]
                    tr = P("tensor", lambda e, jj=jj, pbt=pbt, sl=sl, s=src_fn:
                           e.transpose(
                        pbt[:, sl: sl + 128], s(jj), id_b[:]),
                        after=list(after_src) + war +
                        (list(extra_first) if jj == 0 else []))
                    
                    ev = P("vector", lambda e, jj=jj, pbt=pbt, sl=sl, d=dst_fn:
                           e.tensor_copy(d(jj), pbt[:, sl: sl + 128]),
                           after=[tr])
                    
                    tr_evs.append(ev)
                    evs.append(ev)
                return evs

            for ch in range(nch):
                t0 = ch * TC
                par = ch % 2
                x_tile = x_tile2[:, par * D:(par + 1) * D]
                lnx_t = lnx_t2[:, par * D:(par + 1) * D]
                lnx_ready = []
                for q in range(NT):
                    lq = P("sync", lambda e, q=q, t0=t0, x_tile=x_tile:
                           e.dma_start(
                        out=x_tile, in_=xb[t0 + q * 128: t0 + (q + 1) * 128, :]),
                        dma=True, dsem="xt",
                        after=prev_apply + (prev_chunk_tail if q == 0 else []))
                    m_col, r_col, sdone = ln_stats(x_tile, [lq])
                    ap = P("vector", lambda e, m=m_col, r=r_col, lnx_t=lnx_t, x_tile=x_tile: e.tensor_scalar(
                        lnx_t, x_tile, m, r, ALU.subtract, ALU.mult),
                        after=sdone + tr_evs[-1:])
                    prev_apply = [ap]
                    if q == 0:
                        dump("lnt", lnx_t, [ap])
                    evs = transpose_block(
                        lambda jj, lnx_t=lnx_t: lnx_t[:, jj * 128:(jj + 1) * 128],
                        lambda jj, q=q: lnx_d[:, jj * TC + q * 128:
                                              jj * TC + q * 128 + 128],
                        [ap])
                    if q == 0 and ch == 0:
                        dump("tr0", lnx_d[:], [evs[0]])
                        dump("tr8", lnx_d[:], [evs[-1]])
                    lnx_ready += evs
                dump("lnx", lnx_d[:], [lnx_ready[-1]])
                if ch == 1 and dbg == "tails1":
                    dump("tails1", tails[:], [lnx_ready[-1]])

                conv_done = []
                silu_done = []
                for g in range(NG):
                    xidx = ch * NG + g
                    lx_w = P("sync", lambda e, g=g, xidx=xidx: e.dma_start(
                        out=xw_ring[:, (xidx % 3) * 1024: (xidx % 3) * 1024 + 1024],
                        in_=w_x_dram[g]), dma=True, dsem="xw%d" % (xidx % 3),
                        after=(xw_mms[xidx - 3: xidx - 2] if xidx >= 3
                               else win_done))
                    xw_loads.append(lx_w)
                    pz = [pb0, pb1][g % 2]

                    def fmm(e, g=g, pz=pz, xidx=xidx):
                        r = None
                        for k in range(ND):
                            r = e.matmul(pz[:, 0:TC],
                                         xw_ring[:, (xidx % 3) * 1024 + k * 128:
                                                 (xidx % 3) * 1024 + k * 128 + 128],
                                         lnx_d[:, k * TC:(k + 1) * TC],
                                         start=(k == 0), stop=(k == ND - 1))
                        return r
                    mm = P("tensor", fmm,
                           after=[lnx_ready[-1], lx_w] + conv_done[-2:-1] +
                           ([m_done[-1]] if ch == 0 and g == 0 else []))
                    xw_mms.append(mm)
                    evac = P("scalar", lambda e, pz=pz: e.copy(
                        xi_ext[:, 3:3 + TC], pz[:, 0:TC]),
                        after=[mm] + conv_done[-1:])
                    if ch == 0 and g == 0:
                        dump("xi0", xi_ext[:, 3:TC + 3], [evac])
                    if ch == 0 and g == 1:
                        dump("xi1", xi_ext[:, 3:TC + 3], [evac])
                    tl = tails[:, g * 4: g * 4 + 4]
                    k0 = P("vector", lambda e, tl=tl: e.tensor_copy(
                        xi_ext[:, 0:3], tl[:, 0:3]), after=conv_done[-1:])
                    k2 = P("vector", lambda e, tl=tl: e.tensor_copy(
                        tl[:, 0:3], xi_ext[:, TC:TC + 3]), after=[evac, k0])
                    acc = conv_acc[:, 0:TC]
                    cw = lambda j, g=g: convw_sb[:, g * 4 + j: g * 4 + j + 1]
                    c0 = P("vector", lambda e, acc=acc, cw=cw:
                           e.tensor_scalar_mul(acc, xi_ext[:, 3:3 + TC],
                                               cw(3)), after=[evac, k0])
                    c1 = P("vector", lambda e, acc=acc, cw=cw:
                           e.scalar_tensor_tensor(acc, xi_ext[:, 2:2 + TC],
                                                  cw(2), acc,
                                                  ALU.mult, ALU.add),
                           after=[c0])
                    c2 = P("vector", lambda e, acc=acc, cw=cw:
                           e.scalar_tensor_tensor(acc, xi_ext[:, 1:1 + TC],
                                                  cw(1), acc,
                                                  ALU.mult, ALU.add),
                           after=[c1])
                    c3 = P("vector", lambda e, acc=acc, cw=cw:
                           e.scalar_tensor_tensor(acc, xi_ext[:, 0:TC],
                                                  cw(0), acc,
                                                  ALU.mult, ALU.add),
                           after=[c2])
                    if ch == 0 and g == 0:
                        dump("cacc0", conv_acc[:, 0:TC], [c3])
                    sil = P("scalar", lambda e, g=g: e.activation(
                        xc_bf[:, g * TC:(g + 1) * TC], conv_acc[:, 0:TC],
                        AF.Silu, bias=convb_sb[:, g:g + 1]), after=[c3, k2])
                    conv_done.append(sil)
                dump("xc", xc_bf[:], [conv_done[-1]])
                dump("tails0", tails[:], [conv_done[-1]])

                def fxdbl(e):
                    r = None
                    for k in range(NG):
                        r = e.matmul(pb2[0:96, 0:TC],
                                     w_xT_bf[:, k * 96:(k + 1) * 96],
                                     xc_bf[:, k * TC:(k + 1) * TC],
                                     start=(k == 0), stop=(k == NG - 1))
                    return r
                xdbl = P("tensor", fxdbl, after=[conv_done[-1]] +
                         ([c_wx] if ch == 0 else []))
                ev_dtr = P("scalar", lambda e: e.copy(dtr_bf[:], pb2[0:64, 0:TC]),
                           after=[xdbl])
                ev_bc = P("scalar", lambda e: e.copy(bc_bf[:], pb2[64:96, 0:TC]),
                          after=[xdbl])
                d_out = P("sync", lambda e: e.dma_start(out=bc_stage[:],
                                                        in_=bc_bf[:]),
                          dma=True, dsem="bco", after=[ev_bc] + prev_crep +
                          ([c_wdt] if ch == 0 else []))
                d_brep_a = P("sync", lambda e: e.dma_start(
                    out=B_rep()[:, 0:8 * TC].rearrange(
                        "p (s t) -> p s t", s=8),
                    in_=bc_stage[0:8, :].unsqueeze(0).broadcast_to((128, 8, TC))),
                    dma=True, dsem="bcb", after=[d_out])
                d_brep = P("sync", lambda e: e.dma_start(
                    out=B_rep()[:, 8 * TC:16 * TC].rearrange(
                        "p (s t) -> p s t", s=8),
                    in_=bc_stage[8:16, :].unsqueeze(0).broadcast_to((128, 8, TC))),
                    dma=True, dsem="bcb1", after=[d_out])
                d_crep = P("sync", lambda e: e.dma_start(
                    out=C_rep().rearrange("p (s t) -> p s t", s=S),
                    in_=bc_stage[16:32, :].unsqueeze(0).broadcast_to((128, S, TC))),
                    dma=True, dsem="bcc", after=[d_out])
                prev_crep = [d_crep]

                dump("dtr", dtr_bf[:], [ev_dtr])
                dump("bc", bc_bf[:], [ev_bc])
                dump("brep", B_rep(), [d_brep])

                yprime_done = []
                prev_sil = []
                prev_soft = []
                prev_scan = []
                prev_yz = []
                prev_pr = []
                prev_a1 = []
                prev_a4 = []
                prev_uprep = []
                dtv_l = {}
                dtx_l = {}

                def dt_chain(g):
                    pdt = [pb3, pb4][g % 2]
                    dmm = P("tensor", lambda e, g=g, pdt=pdt: e.matmul(
                        pdt[:, 0:TC], w_dtT_bf[:, g * 128:(g + 1) * 128],
                        dtr_bf[:], start=True, stop=True),
                        after=[ev_dtr] + prev_soft[-2:-1] +
                        ([c_wdt] if ch == 0 and g == 0 else []))
                    sp1 = P("scalar", lambda e, g=g, pdt=pdt: e.activation(
                        scr_f[:, 0:TC], pdt[:, 0:TC], AF.Exp,
                        bias=bdt_sb[:, g:g + 1]),
                        after=[dmm])
                    dtv = dt_b[:, (g % 2) * TC:(g % 2) * TC + TC]
                    war_dt = [prev_scan[-1]] if g >= 2 else []
                    sp2 = P("scalar", lambda e, dtv=dtv: e.activation(
                        dtv, scr_f[:, 0:TC], AF.Ln, bias=1.0),
                        after=[sp1] + war_dt)
                    prev_soft.append(sp2)
                    dtx = P("gpsimd", lambda e, g=g, dtv=dtv: e.tensor_mul(
                        dtx_bf[:], dtv, xc_bf[:, g * TC:(g + 1) * TC]),
                        after=[sp2] + prev_uprep[-1:])
                    dtv_l[g] = dtv
                    dtx_l[g] = dtx

                dt_chain(0)
                if ch == 0 and dbg == "dt":
                    dump("dt", dt_b[:, 0:TC], [prev_soft[0]])
                for g in range(NG):
                    if g % 4 == 0:
                        zbanks = [pb0, pb1, pb2, pb6]
                        for gq in range(g, g + 4):
                            zidx = ch * NG + gq
                            lz_w = P("sync", lambda e, gq=gq, zidx=zidx:
                                     e.dma_start(
                                out=zw_ring[:, (zidx % 3) * 1024:
                                            (zidx % 3) * 1024 + 1024],
                                in_=w_z_dram[gq]), dma=True,
                                dsem="zw%d" % (zidx % 3),
                                after=(zw_mms[zidx - 3: zidx - 2] if zidx >= 3
                                       else win_done))
                            zw_loads.append(lz_w)
                            pzq = zbanks[gq % 4]

                            def fz(e, gq=gq, pzq=pzq, zidx=zidx):
                                r = None
                                for k in range(ND):
                                    r = e.matmul(
                                        pzq[:, 0:TC],
                                        zw_ring[:, (zidx % 3) * 1024 + k * 128:
                                                (zidx % 3) * 1024 + k * 128 + 128],
                                        lnx_d[:, k * TC:(k + 1) * TC],
                                        start=(k == 0), stop=(k == ND - 1))
                                return r
                            if g == 0:
                                zwar = [conv_done[-1]]
                                if gq % 4 == 2:
                                    zwar += [ev_dtr, ev_bc]
                            else:
                                zwar = prev_sil[-4:-3]
                            zmm = P("tensor", fz, after=zwar + [lz_w])
                            zw_mms.append(zmm)
                            szv = sz_b[:, (gq % 4) * TC:(gq % 4) * TC + TC]
                            war_sz = [prev_yz[-4]] if len(prev_yz) >= 4 else []
                            sil = P("scalar", lambda e, szv=szv, pzq=pzq:
                                    e.activation(szv, pzq[:, 0:TC], AF.Silu),
                                    after=[zmm] + war_sz)
                            prev_sil.append(sil)
                    szv = sz_b[:, (g % 4) * TC:(g % 4) * TC + TC]
                    dtv = dtv_l[g]
                    up_a = P("vector", lambda e: e.tensor_tensor(
                        u_v()[:, 0:8 * TC].rearrange("p (s t) -> p s t", s=8),
                        dtx_bf[:].unsqueeze(1).broadcast_to((128, 8, TC)),
                        B_rep()[:, 0:8 * TC].rearrange("p (s t) -> p s t", s=8),
                        ALU.mult), after=[dtx_l[g], d_brep_a] + prev_pr[-1:] +
                        ([m_done[-1]] if ch == 0 and g == 0 else []))
                    uprep = P("vector", lambda e: e.tensor_tensor(
                        u_v()[:, 8 * TC:16 * TC].rearrange(
                            "p (s t) -> p s t", s=8),
                        dtx_bf[:].unsqueeze(1).broadcast_to((128, 8, TC)),
                        B_rep()[:, 8 * TC:16 * TC].rearrange(
                            "p (s t) -> p s t", s=8),
                        ALU.mult), after=[dtx_l[g], d_brep] + prev_pr[-1:] +
                        ([m_done[-1]] if ch == 0 and g == 0 else []))
                    prev_uprep.append(uprep)
                    if g == 0:
                        dump("u", u_v()[:], [uprep])
                    scans = []
                    for s in range(S):
                        dav = da_b[:, (s % 4) * TC:(s % 4) * TC + TC]
                        war_da = [prev_dA[s % 4]] if (s % 4) in prev_dA else []
                        dexp = P("scalar", lambda e, g=g, s=s, dav=dav, dtv=dtv:
                                 e.activation(dav, dtv, AF.Exp,
                                              scale=a_sb[:, g * S + s:
                                                         g * S + s + 1]),
                                 after=[prev_soft[g]] + war_da)
                        sc = P("vector", lambda e, g=g, s=s, dav=dav:
                               e.tensor_tensor_scan(
                                   u_v()[:, s * TC:(s + 1) * TC], dav,
                                   u_v()[:, s * TC:(s + 1) * TC],
                                   carry[:, g * S + s: g * S + s + 1],
                                   ALU.mult, ALU.add),
                               after=[dexp, up_a if s < 8 else uprep])
                        prev_dA[s % 4] = sc
                        scans.append(sc)
                    prev_scan.append(scans[-1])
                    cu = P("vector", lambda e, g=g: e.tensor_copy(
                        carry[:, g * S:(g + 1) * S].unsqueeze(-1),
                        u_v().rearrange("p (s t) -> p s t", s=S)[:, :, TC - 1:TC]),
                        after=[scans[-1]])
                    if g == 0:
                        dump("scan", u_v()[:], [cu])
                    if g + 1 < NG:
                        dt_chain(g + 1)
                    pr = P("vector", lambda e: e.tensor_tensor(
                        prb()[:], u_v()[:], C_rep()[:], ALU.mult),
                        after=[cu, d_crep] + prev_a4[-1:] +
                        ([m_done[-1]] if ch == 0 and g == 0 else []))
                    prev_pr.append(pr)
                    a1 = P("vector", lambda e: e.tensor_add(
                        prb()[:, 0:8 * TC], prb()[:, 0:8 * TC],
                        prb()[:, 8 * TC:16 * TC]), after=[pr])
                    prev_a1.append(a1)
                    a2 = P("vector", lambda e: e.tensor_add(
                        prb()[:, 0:4 * TC], prb()[:, 0:4 * TC],
                        prb()[:, 4 * TC:8 * TC]), after=[a1])
                    a3 = P("vector", lambda e: e.tensor_add(
                        prb()[:, 0:2 * TC], prb()[:, 0:2 * TC],
                        prb()[:, 2 * TC:4 * TC]), after=[a2])
                    a4 = P("vector", lambda e: e.tensor_add(
                        y_f[:], prb()[:, 0:TC], prb()[:, TC:2 * TC]),
                        after=[a3] + prev_yz[-1:])
                    prev_a4.append(a4)
                    yz = P("vector", lambda e, szv=szv: e.tensor_mul(
                        yz_f[:], y_f[:], szv), after=[a4, prev_sil[g]])
                    prev_yz.append(yz)
                    yp2 = P("vector", lambda e, g=g: e.scalar_tensor_tensor(
                        xc_bf[:, g * TC:(g + 1) * TC],
                        xc_bf[:, g * TC:(g + 1) * TC],
                        d_sb[:, g:g + 1], yz_f[:], ALU.mult, ALU.add),
                        after=[yz, dtx_l[g]])
                    yprime_done.append(yp2)
                    if g == 0:
                        dump("y", y_f[:], [a4])
                        dump("yp", xc_bf[:, 0:TC], [yp2])

                wo_ev = []
                for m in range(ND):
                    widx = ch * ND + m
                    lw = P("sync", lambda e, m=m, widx=widx: e.dma_start(
                        out=wo_ring[:, (widx % 2) * 2048: (widx % 2) * 2048 + 2048],
                        in_=w_o_dram[m]), dma=True, dsem="wo%d" % (widx % 2),
                        after=(wo_mms[widx - 2: widx - 1] if widx >= 2
                               else wo_done))
                    wo_loads.append(lw)
                    pz = [pb0, pb1][m % 2]

                    def fwo(e, m=m, pz=pz, widx=widx):
                        r = None
                        for k in range(NG):
                            r = e.matmul(pz[:, 0:TC],
                                         wo_ring[:, (widx % 2) * 2048 + k * 128:
                                                 (widx % 2) * 2048 + k * 128 + 128],
                                         xc_bf[:, k * TC:(k + 1) * TC],
                                         start=(k == 0), stop=(k == NG - 1))
                        return r
                    mm = P("tensor", fwo,
                           after=[yprime_done[-1], prev_sil[-1], lw,
                                  prev_sil[-4], prev_sil[-3]] + wo_ev[-2:-1])
                    wo_mms.append(mm)
                    war_mo = [d_brep] if m == 0 else []
                    ev = P("scalar", lambda e, m=m, pz=pz: e.copy(
                        mo_d()[:, m * TC:(m + 1) * TC], pz[:, 0:TC]),
                        after=[mm] + war_mo + prev_pr[-1:])
                    wo_ev.append(ev)
                dump("mo", mo_d()[:], [wo_ev[-1]])

                mo_t_ev = []
                for q in range(NT):
                    evs = transpose_block(
                        lambda jj, q=q: mo_d()[:, jj * TC + q * 128:
                                               jj * TC + q * 128 + 128],
                        lambda jj, q=q: mo_t()[:, q * D + jj * 128:
                                               q * D + jj * 128 + 128],
                        [wo_ev[-1]],
                        extra_first=([d_crep, prev_pr[-1]] if q == 0 else []))
                    mo_t_ev += evs
                dump("mot", mo_t(), [mo_t_ev[-1]])

                ln3_ready = []
                prev_ln3ap = []
                apply7 = None
                for q in range(NT):
                    prev_stores_q = (store_ops[-4 + q:][:1]
                                     if len(store_ops) >= 4 else [])
                    movq = mo_t()[:, q * D:(q + 1) * D]
                    m2, r2, sdone = ln_stats(movq, [mo_t_ev[-1]])
                    lx = P("sync", lambda e, q=q, t0=t0, x_tile=x_tile:
                           e.dma_start(
                        out=x_tile, in_=xb[t0 + q * 128: t0 + (q + 1) * 128, :]),
                        dma=True, dsem="xt", after=prev_apply)
                    x2q = x2_v()[:, q * D:(q + 1) * D]
                    sv = scr_f[:, D:2 * D]
                    aa0 = P("vector", lambda e, movq=movq, m=m2, r=r2, sv=sv:
                            e.tensor_scalar(sv, movq, m, r,
                                            ALU.subtract, ALU.mult),
                            after=sdone + prev_apply)
                    if ch == 0 and q == 0:
                        dump("sv0", sv, [aa0])
                    aa = P("vector", lambda e, x2q=x2q, sv=sv, x_tile=x_tile: e.tensor_add(
                        x2q, sv, x_tile), after=[aa0, lx] + prev_stores_q)
                    if ch == 0 and q == 0:
                        dump("xq0", x_tile, [aa])
                        dump("x2q0", x2_v()[:, 0:D], [aa])
                    prev_apply = [aa]
                    apply7 = aa
                    m3, r3, s3done = ln_stats(x2q, [aa])
                    ap3 = P("vector", lambda e, x2q=x2q, m3=m3, r3=r3, lnx_t=lnx_t:
                            e.tensor_scalar(lnx_t, x2q, m3, r3,
                                            ALU.subtract, ALU.mult),
                            after=s3done + prev_ln3ap + tr_evs[-1:])
                    prev_ln3ap = [ap3]
                    if ch == 0 and q == 0:
                        dump("ln3t", lnx_t, [ap3])
                    evs = transpose_block(
                        lambda jj, lnx_t=lnx_t: lnx_t[:, jj * 128:(jj + 1) * 128],
                        lambda jj, q=q: lnx_d[:, jj * TC + q * 128:
                                              jj * TC + q * 128 + 128],
                        [ap3])
                    if ch == 0 and q == 0:
                        dump("ln3e", lnx_d[:], [evs[-1]])
                    ln3_ready += evs
                dump("x2", x2_v()[:], [prev_apply[0], prev_ln3ap[0]])
                dump("ln3", lnx_d[:], [ln3_ready[-1]])

                ffn_ev = []
                for m in range(ND):
                    pz = [pb0, pb1][m % 2]

                    def fff(e, m=m, pz=pz):
                        r = None
                        for k in range(ND):
                            r = e.matmul(pz[:, 0:TC],
                                         m_bf[:, k * D + m * 128:
                                              k * D + m * 128 + 128],
                                         lnx_d[:, k * TC:(k + 1) * TC],
                                         start=(k == 0), stop=(k == ND - 1))
                        return r
                    mm = P("tensor", fff, after=[ln3_ready[-1]] + ffn_ev[-2:-1])
                    ev = P("scalar", lambda e, m=m, pz=pz: e.activation(
                        mo_d()[:, m * TC:(m + 1) * TC], pz[:, 0:TC], AF.Identity,
                        bias=bias_sb[:, m:m + 1]),
                        after=[mm] + (mo_t_ev[-1:] if m == 0 else []))
                    ffn_ev.append(ev)
                dump("ffn", mo_d()[:], [ffn_ev[-1]])

                ffn_t_ev = []
                for q in range(NT):
                    evs = transpose_block(
                        lambda jj, q=q: mo_d()[:, jj * TC + q * 128:
                                               jj * TC + q * 128 + 128],
                        lambda jj, q=q: mo_t()[:, q * D + jj * 128:
                                               q * D + jj * 128 + 128],
                        [ffn_ev[-1]],
                        extra_first=([prev_ln3ap[0]] if q == 0 else []))
                    ffn_t_ev += evs
                dump("ffnt", mo_t(), [ffn_t_ev[-1]])

                chunk_tail = []
                for q in range(NT):
                    fq = mo_t()[:, q * D:(q + 1) * D]
                    x2q = x2_v()[:, q * D:(q + 1) * D]
                    m4, r4, s4done = ln_stats(fq, [ffn_t_ev[-1]])
                    sv = scr_f[:, D:2 * D]
                    f0 = P("vector", lambda e, fq=fq, m=m4, r=r4, sv=sv:
                           e.tensor_scalar(sv, fq, m, r,
                                           ALU.subtract, ALU.mult),
                           after=s4done + prev_apply)
                    fin = P("vector", lambda e, x2q=x2q, sv=sv: e.tensor_add(
                        x2q, sv, x2q), after=[f0])
                    prev_apply = [fin]
                    if q == NT - 1:
                        dump("fout", x2_v(), [fin])
                    st = P("sync", lambda e, q=q, x2q=x2q, t0=t0:
                           e.dma_start(
                        out=yb[t0 + q * 128: t0 + (q + 1) * 128, :], in_=x2q),
                        dma=True, dsem="st%d" % q, after=[fin])
                    store_ops.append(st)
                    chunk_tail.append(fin)
                prev_chunk_tail = chunk_tail + store_ops[-4:]

            P("sync", lambda e: e.nop(), after=store_ops[-4:])
        except _StopBuild:
            pass

        plan.finalize()

        @block.sync
        def _(eng):
            plan.emit_engine("sync", eng, sems, dsems)
            for n, c in sorted(plan.dsem_counts.items()):
                eng.wait_ge(dsems[n], c)
            for n, c in plan.final_counts.items():
                if c and n != "sync":
                    eng.wait_ge(sems[n], c)

        @block.gpsimd
        def _(eng):
            plan.emit_engine("gpsimd", eng, sems, dsems)

        @block.tensor
        def _(eng):
            plan.emit_engine("tensor", eng, sems, dsems)

        @block.vector
        def _(eng):
            plan.emit_engine("vector", eng, sems, dsems)

        @block.scalar
        def _(eng):
            plan.emit_engine("scalar", eng, sems, dsems)

    nc._dbg_plan = plan
    return nc


def _host_prep(inp):
    f32 = np.float32
    x = np.ascontiguousarray(np.asarray(inp["L0_attn_x"], f32))
    W_in = np.asarray(inp["L0_attn_W_in"], f32)
    conv_w = np.asarray(inp["L0_attn_conv_w"], f32).reshape(E, 4)
    conv_b = np.asarray(inp["L0_attn_conv_b"], f32)
    W_x = np.asarray(inp["L0_attn_W_x"], f32)
    W_dt = np.asarray(inp["L0_attn_W_dt"], f32)
    b_dt = np.asarray(inp["L0_attn_b_dt"], f32)
    A_log = np.asarray(inp["L0_attn_A_log"], f32)
    Dv = np.asarray(inp["L0_attn_D"], f32)
    W_out = np.asarray(inp["L0_attn_W_out"], f32)
    wr = np.asarray(inp["L0_ffn_w_freq_real"], f32)
    wi = np.asarray(inp["L0_ffn_w_freq_imag"], f32)
    bias = np.asarray(inp["L0_ffn_bias"], f32)

    k = np.arange(D)
    ang = ((np.outer(k, k) % D).astype(np.float64)) * (2.0 * np.pi / D)
    Fr = np.cos(ang)
    Fi = -np.sin(ang)
    Gr = np.cos(ang) / D
    Gi = np.sin(ang) / D

    pl = lambda v: np.ascontiguousarray(v.reshape(NG, 128).T)
    base = {
        "w_inT": np.ascontiguousarray(W_in.T),
        "w_xT": np.ascontiguousarray(W_x.T),
        "w_dtT": np.ascontiguousarray(W_dt.T),
        "w_outT": np.ascontiguousarray(W_out.T),
        "convw_l": np.ascontiguousarray(
            conv_w.reshape(NG, 128, 4).transpose(1, 0, 2).reshape(128, NG * 4)),
        "convb_l": pl(conv_b),
        "bdt_l": pl(b_dt),
        "d_l": pl(Dv),
        "alog_l": np.ascontiguousarray(
            A_log.reshape(NG, 128, S).transpose(1, 0, 2).reshape(128, NG * S)),
        "wfr": np.ascontiguousarray(wr),
        "wfi": np.ascontiguousarray(wi),
        "bias_l": np.ascontiguousarray(bias.reshape(ND, 128).T),
        "fr_c": Fr.astype(BF),
        "fi_c": Fi.astype(BF),
        "fineg_c": (-Fi).astype(BF),
        "gr_c": Gr.astype(BF),
        "gineg_c": (-Gi).astype(BF),
        "idb_c": np.eye(128).astype(BF),
    }
    return x, base


def kernel(**inp):
    x, base = _host_prep(inp)
    nc = build_nc()
    in_maps = [dict(base, xb=np.ascontiguousarray(x[b])) for b in range(8)]
    res = run_bass_kernel_spmd(nc, in_maps, core_ids=list(range(8)))
    out = np.stack([np.asarray(res.results[b]["yb"]) for b in range(8)], axis=0)
    return out.astype(np.float32)


# version 37 (final)
# speedup vs baseline: 1.0904x; 1.0904x over previous
import numpy as np
import ml_dtypes
import concourse.bass as bass
import concourse.mybir as mybir
from concourse.bass_utils import run_bass_kernel_spmd

F32 = mybir.dt.float32
BF16 = mybir.dt.bfloat16
AF = mybir.ActivationFunctionType
ALU = mybir.AluOpType
AX = mybir.AxisListType

T = 4096
D = 1024
E = 2048
S = 16
R = 64
TC = 512
NCH = T // TC
NG = E // 128
ND = D // 128
NT = TC // 128
EPS = 1e-5
BF = ml_dtypes.bfloat16


class Plan:

    ENGINES = ("sync", "gpsimd", "tensor", "vector", "scalar")

    def __init__(self):
        self.ops = []
        self.dsem_counts = {}

    def add(self, engine, fn, after=(), dma=False, dsem=None):
        idx = len(self.ops)
        for j in after:
            assert 0 <= j < idx, (j, idx)
        assert (dsem is not None) == bool(dma)
        self.ops.append(dict(engine=engine, fn=fn, after=list(after),
                             dma=dma, dsem=dsem, marked=False, val=None))
        return idx

    def finalize(self):
        for op in self.ops:
            if op["dma"]:
                op["marked"] = True
            for j in op["after"]:
                pj = self.ops[j]
                if pj["engine"] != op["engine"] or pj["dma"]:
                    pj["marked"] = True
        cnt = {e: 0 for e in self.ENGINES}
        for op in self.ops:
            if not op["marked"]:
                continue
            if op["dma"]:
                c = self.dsem_counts.get(op["dsem"], 0) + 16
                self.dsem_counts[op["dsem"]] = c
                op["val"] = c
            else:
                cnt[op["engine"]] += 1
                op["val"] = cnt[op["engine"]]
        self.final_counts = cnt

    def emit_engine(self, ename, eng, sems, dsems):
        for op in self.ops:
            if op["engine"] != ename:
                continue
            waits = {}
            for j in op["after"]:
                pj = self.ops[j]
                if pj["dma"]:
                    k = ("d", pj["dsem"])
                elif pj["engine"] == ename:
                    continue
                else:
                    k = ("e", pj["engine"])
                waits[k] = max(waits.get(k, 0), pj["val"])
            for (kind, s), v in waits.items():
                eng.wait_ge(dsems[s] if kind == "d" else sems[s], v)
            inst = op["fn"](eng)
            if op["marked"]:
                assert inst is not None, "marked op returned no instruction"
                if op["dma"]:
                    inst.then_inc(dsems[op["dsem"]], 16)
                else:
                    inst.then_inc(sems[ename], 1)


class _StopBuild(Exception):
    pass


def build_nc(nch=NCH, dbg=None):
    nc = bass.Bass("TRN2", target_bir_lowering=False)

    di = lambda n, sh, dt=F32: nc.dram_tensor(n, sh, dt, kind="ExternalInput").ap()
    xb = di("xb", [T, D])
    w_inT = di("w_inT", [D, 2 * E])
    w_xT = di("w_xT", [E, 96])
    w_dtT = di("w_dtT", [R, E])
    w_outT = di("w_outT", [E, D])
    convw_l = di("convw_l", [128, NG * 4])
    convb_l = di("convb_l", [128, NG])
    bdt_l = di("bdt_l", [128, NG])
    d_l = di("d_l", [128, NG])
    alog_l = di("alog_l", [128, NG * S])
    wfr = di("wfr", [D, D])
    wfi = di("wfi", [D, D])
    bias_l = di("bias_l", [128, ND])
    fr_c = di("fr_c", [D, D], BF16)
    fi_c = di("fi_c", [D, D], BF16)
    fineg_c = di("fineg_c", [D, D], BF16)
    gr_c = di("gr_c", [D, D], BF16)
    gineg_c = di("gineg_c", [D, D], BF16)
    idb_c = di("idb_c", [128, 128], BF16)

    yb = nc.dram_tensor("yb", [T, D], F32, kind="ExternalOutput").ap()
    bc_stage = nc.dram_tensor("bc_stage", [32, TC], BF16).ap()
    w_x_dram = nc.dram_tensor("w_x_dram", [NG, 128, ND * 128], BF16).ap()
    w_z_dram = nc.dram_tensor("w_z_dram", [NG, 128, ND * 128], BF16).ap()
    w_o_dram = nc.dram_tensor("w_o_dram", [ND, 128, NG * 128], BF16).ap()

    plan = Plan()
    P = plan.add

    import contextlib
    with contextlib.ExitStack() as _es:
        _e = _es.enter_context
        arena_u8 = _e(nc.sbuf_tensor("arena", [128, 80 * 1024],
                                     mybir.dt.uint8))
        m_bf = _e(nc.sbuf_tensor("m_bf", [128, ND * D], BF16))
        w_xT_bf = _e(nc.sbuf_tensor("w_xT_bf", [128, NG * 96], BF16))
        w_dtT_bf = _e(nc.sbuf_tensor("w_dtT_bf", [64, E], BF16))
        xw_ring = _e(nc.sbuf_tensor("xw_ring", [128, 3 * ND * 128], BF16))
        zw_ring = _e(nc.sbuf_tensor("zw_ring", [128, 3 * ND * 128], BF16))
        wo_ring = _e(nc.sbuf_tensor("wo_ring", [128, 2 * NG * 128], BF16))
        a_sb = _e(nc.sbuf_tensor("a_sb", [128, NG * S], F32))
        convw_sb = _e(nc.sbuf_tensor("convw_sb", [128, NG * 4], F32))
        convb_sb = _e(nc.sbuf_tensor("convb_sb", [128, NG], F32))
        bdt_sb = _e(nc.sbuf_tensor("bdt_sb", [128, NG], F32))
        d_sb = _e(nc.sbuf_tensor("d_sb", [128, NG], F32))
        bias_sb = _e(nc.sbuf_tensor("bias_sb", [128, ND], F32))
        carry = _e(nc.sbuf_tensor("carry", [128, NG * S], F32))
        id_b = _e(nc.sbuf_tensor("id_b", [128, 128], BF16))
        x_tile2 = _e(nc.sbuf_tensor("x_tile2", [128, 2 * D], F32))
        lnx_t2 = _e(nc.sbuf_tensor("lnx_t2", [128, 2 * D], BF16))
        lnx_d = _e(nc.sbuf_tensor("lnx_d", [128, ND * TC], BF16))
        xc_bf = _e(nc.sbuf_tensor("xc_bf", [128, NG * TC], BF16))
        sz_b = _e(nc.sbuf_tensor("sz_b", [128, 4 * TC], BF16))
        dt_b = _e(nc.sbuf_tensor("dt_b", [128, 2 * TC], F32))
        da_b = _e(nc.sbuf_tensor("da_b", [128, 4 * TC], F32))
        conv_acc = _e(nc.sbuf_tensor("conv_acc", [128, TC], F32))
        scr_f = _e(nc.sbuf_tensor("scr_f", [128, 2 * D], F32))
        sq_dump = _e(nc.sbuf_tensor("sq_dump", [128, D], F32))
        xi_ext = _e(nc.sbuf_tensor("xi_ext", [128, TC + 4], F32))
        tails = _e(nc.sbuf_tensor("tails", [128, NG * 4], F32))
        dtr_bf = _e(nc.sbuf_tensor("dtr_bf", [64, TC], BF16))
        bc_bf = _e(nc.sbuf_tensor("bc_bf", [32, TC], BF16))
        dtx_bf = _e(nc.sbuf_tensor("dtx_bf", [128, TC], BF16))
        y_f = _e(nc.sbuf_tensor("y_f", [128, TC], BF16))
        yz_f = _e(nc.sbuf_tensor("yz_f", [128, TC], BF16))
        stats = _e(nc.sbuf_tensor("stats", [128, 128], F32))
        pb0 = _e(nc.psum_tensor("pb0", [128, 512], F32))
        pb1 = _e(nc.psum_tensor("pb1", [128, 512], F32))
        pb2 = _e(nc.psum_tensor("pb2", [128, 512], F32))
        pb3 = _e(nc.psum_tensor("pb3", [128, 512], F32))
        pb4 = _e(nc.psum_tensor("pb4", [128, 512], F32))
        pb5 = _e(nc.psum_tensor("pb5", [128, 1024], BF16))
        pb6 = _e(nc.psum_tensor("pb6", [128, 512], F32))
        pb7 = _e(nc.psum_tensor("pb7", [128, 1024], BF16))
        s_sync = _e(nc.semaphore("s_sync"))
        s_gps = _e(nc.semaphore("s_gps"))
        s_pe = _e(nc.semaphore("s_pe"))
        s_dve = _e(nc.semaphore("s_dve"))
        s_act = _e(nc.semaphore("s_act"))
        DSEM_NAMES = (["c0", "la0", "la1", "la2", "la3",
                       "lb0", "lb1", "lb2", "lb3", "cs0", "cs1", "cs2",
                       "cb0", "cb1", "xw0", "xw1", "xw2", "zw0", "zw1",
                       "zw2", "wo0", "wo1", "xt", "bco", "bcb", "bcc",
                       "st0", "st1", "st2", "st3", "dbg"] +
                      ["w%d%d" % (i, q) for i in range(3) for q in range(2)] +
                      ["q%d%d" % (i, q) for i in range(3) for q in range(4)] +
                      ["A%d%d" % (i, q) for i in range(4) for q in range(2)] +
                      ["B%d%d" % (i, q) for i in range(4) for q in range(2)] +
                      ["bcb1"])
        dsems = {n: _e(nc.semaphore("sd_" + n)) for n in DSEM_NAMES}
        block = _e(nc.Block())
        sems = dict(sync=s_sync, gpsimd=s_gps, tensor=s_pe,
                    vector=s_dve, scalar=s_act)

        def dump(name, ap, after):
            if dbg != name:
                return
            dh = nc.dram_tensor("dbg_" + name, list(ap.shape), ap.dtype,
                                kind="ExternalOutput").ap()
            dd = P("sync", lambda e, ap=ap, dh=dh: e.dma_start(
                out=dh, in_=ap), dma=True, dsem="dbg", after=after)
            P("sync", lambda e: e.nop(), after=[dd])
            raise _StopBuild()
        arena = arena_u8[:]

        def av(off_kb, n_elems, dt=F32):
            bpe = 4 if dt == F32 else (2 if dt == BF16 else 1)
            return arena[:, off_kb * 1024: off_kb * 1024 + n_elems * bpe].bitcast(dt)

        B_rep = lambda: av(0, S * TC, BF16)
        C_rep = lambda: av(16, S * TC, BF16)
        u_v = lambda: av(32, S * TC, BF16)
        prb = lambda: av(48, S * TC, BF16)
        mo_d = lambda: av(0, ND * TC, BF16)
        mo_t = lambda: av(16, NT * D, F32)
        x2_v = lambda: av(64, NT * D, F32)

        try:
            def ld(dst, src, dsem, after=()):
                return P("sync", lambda e, d=dst, s=src: e.dma_start(out=d, in_=s),
                         dma=True, dsem=dsem, after=after)

            ld(convw_sb[:], convw_l[:], "c0")
            ld(convb_sb[:], convb_l[:], "c0")
            ld(bdt_sb[:], bdt_l[:], "c0")
            ld(d_sb[:], d_l[:], "c0")
            ld(a_sb[:], alog_l[:], "c0")
            ld(bias_sb[:], bias_l[:], "c0")
            consts_done = ld(id_b[:], idb_c[:], "c0")

            a_exp = P("scalar", lambda e: e.activation(a_sb[:], a_sb[:], AF.Exp),
                      after=[consts_done])
            a_fix = P("vector", lambda e: e.tensor_scalar_mul(
                a_sb[:], a_sb[:], -1.0), after=[a_exp])
            z_carry = P("gpsimd", lambda e: e.memset(carry[:], 0.0))
            c_eps = P("gpsimd", lambda e: e.memset(stats[:, 120:121], EPS))
            c_half = P("gpsimd", lambda e: e.memset(stats[:, 121:122], -0.5))
            z_tails = P("gpsimd", lambda e: e.memset(tails[:], 0.0))

            wr_bf = lambda: av(0, ND * D, BF16)
            wi_bf = lambda: av(16, ND * D, BF16)
            htr = lambda: av(32, ND * D, BF16)
            hti = lambda: av(48, ND * D, BF16)
            ca_v = lambda p: av(64 + 2 * (p % 4), D, BF16)
            cb_v = lambda p: av(72 + 2 * (p % 4), D, BF16)

            wdeps = []
            j = 0
            for i in range(ND):
                for which, dst in ((wfr, wr_bf), (wfi, wi_bf)):
                    tmp = av(32 + 8 * (j % 3), D, F32)
                    war = [wdeps[j - 3]] if j >= 3 else []
                    ls = [P(["sync", "gpsimd"][q],
                            lambda e, i=i, t=tmp, w=which, q=q: e.dma_start(
                        out=t[:, q * 512:(q + 1) * 512],
                        in_=w[i * 128:(i + 1) * 128, q * 512:(q + 1) * 512]),
                        dma=True, dsem="w%d%d" % (j % 3, q), after=war)
                        for q in range(2)]
                    c1 = P("vector", lambda e, i=i, t=tmp, dst=dst:
                           e.tensor_copy(
                        dst()[:, i * D:(i + 1) * D], t[:]), after=ls)
                    wdeps.append(c1)
                    j += 1

            def cplx_pass(wA_fn, constA, wB_fn, constB, out_sb, out_stride,
                          extra_after):
                res = []
                prev_evacs = []
                for mp in range(ND // 2):
                    kdeps = []
                    for k in range(ND):
                        lwar = (kdeps[-3:-2] if k >= 3 else
                                (([] if extra_after and
                                  extra_after[0] == wdeps[-1]
                                  else extra_after)
                                 if mp == 0 else prev_evacs[-1:]))
                        la_ = [P(["sync", "gpsimd"][q],
                                 lambda e, k=k, q=q: e.dma_start(
                            out=ca_v(k % 4)[:, q * 512:(q + 1) * 512],
                            in_=constA[k * 128:(k + 1) * 128,
                                       q * 512:(q + 1) * 512]),
                            dma=True, dsem="A%d%d" % (k % 4, q), after=lwar)
                            for q in range(2)]
                        lb_ = [P(["scalar", "gpsimd"][q],
                                 lambda e, k=k, q=q: e.dma_start(
                            out=cb_v(k % 4)[:, q * 512:(q + 1) * 512],
                            in_=constB[k * 128:(k + 1) * 128,
                                       q * 512:(q + 1) * 512]),
                            dma=True, dsem="B%d%d" % (k % 4, q), after=lwar)
                            for q in range(2)]
                        war = (prev_evacs[4 * (mp - 1):4 * (mp - 1) + 4]
                               if mp >= 1 and k == 0 else [])

                        def fmm(e, mp=mp, k=k):
                            r = None
                            for m in (2 * mp, 2 * mp + 1):
                                pA = [pb0, pb1][m % 2]
                                pB = [pb2, pb3][m % 2]
                                for nt, pp in ((0, pA), (1, pB)):
                                    e.matmul(pp[:],
                                             wA_fn()[:, k * D + m * 128:
                                                     k * D + m * 128 + 128],
                                             ca_v(k % 4)[:, nt * 512: nt * 512 + 512],
                                             start=(k == 0), stop=False)
                                    r = e.matmul(pp[:],
                                                 wB_fn()[:, k * D + m * 128:
                                                         k * D + m * 128 + 128],
                                                 cb_v(k % 4)[:, nt * 512: nt * 512 + 512],
                                                 start=False, stop=(k == ND - 1))
                            return r
                        wk = (wdeps[2 * k:2 * k + 2]
                              if (extra_after and extra_after[0] == wdeps[-1])
                              else [])
                        mmk = P("tensor", fmm, after=la_ + lb_ + war + wk)
                        kdeps.append(mmk)
                    for m in (2 * mp, 2 * mp + 1):
                        pA = [pb0, pb1][m % 2]
                        pB = [pb2, pb3][m % 2]
                        for nt, pp in ((0, pA), (1, pB)):
                            ev = P("vector", lambda e, m=m, nt=nt, pp=pp:
                                   e.tensor_copy(
                                out_sb()[:, m * out_stride + nt * 512:
                                         m * out_stride + nt * 512 + 512],
                                pp[:]), after=[kdeps[-1]])
                            prev_evacs.append(ev)
                    res.append(prev_evacs[-1])
                return res

            htr_done = cplx_pass(wr_bf, fr_c, wi_bf, fineg_c, htr, D,
                                 [wdeps[-1]])
            hti_done = cplx_pass(wr_bf, fi_c, wi_bf, fr_c, hti, D,
                                 [htr_done[-1]])
            m_done = cplx_pass(htr, gr_c, hti, gineg_c,
                               lambda: m_bf[:], D, [hti_done[-1]])
            dump("m", m_bf[:], [m_done[-1]])

            cast_sl = lambda p: av(8 * (p % 3), 2048, F32)
            cast_bf = lambda p: av(24 + 4 * (p % 2), 2048, BF16)

            def cast_stream(src_rows_fn, n_k, n_half, out_dmas_fn, after):
                deps = list(after)
                casts = []
                dd_groups = []
                j = 0
                for k in range(n_k):
                    for h in range(n_half):
                        sl = cast_sl(j)
                        war = [casts[j - 3]] if j >= 3 else deps
                        ncols = src_rows_fn(k, h).shape[1]
                        qw = ncols // 4
                        lds = [P(["sync", "gpsimd", "scalar", "gpsimd"][q],
                                 lambda e, k=k, h=h, sl=sl, qw=qw, q=q:
                                 e.dma_start(
                                     out=sl[:, q * qw:(q + 1) * qw],
                                     in_=src_rows_fn(k, h)[:, q * qw:
                                                           (q + 1) * qw]),
                                 dma=True, dsem="q%d%d" % (j % 3, q),
                                 after=war) for q in range(4)]
                        bfw = dd_groups[j - 2] if j >= 2 else deps
                        c = P("vector", lambda e, j=j, sl=sl, ncols=ncols:
                              e.tensor_copy(cast_bf(j)[:, 0:ncols],
                                            sl[:, 0:ncols]),
                              after=lds + bfw)
                        casts.append(c)
                        dds = []
                        for od_out, od_cols in out_dmas_fn(k, h):
                            ng = od_out.shape[0]
                            dds.append(P(
                                "sync",
                                lambda e, o=od_out, cc=od_cols, j=j, ng=ng:
                                e.dma_start(
                                    out=o.rearrange("g p c -> p g c"),
                                    in_=cast_bf(j)[:, cc[0]:cc[1]].rearrange(
                                        "p (g c) -> p g c", g=ng)),
                                dma=True, dsem="cb%d" % (j % 2), after=[c]))
                        dd_groups.append(dds)
                        j += 1
                return dd_groups[-1] + dd_groups[-2]

            def win_src(k, h):
                return w_inT[k * 128:(k + 1) * 128, h * E:(h + 1) * E]

            def win_out(k, h):
                tgt = w_x_dram if h == 0 else w_z_dram
                return [(tgt[jj0:jj0 + 8, :, k * 128:(k + 1) * 128],
                         (jj0 * 128, (jj0 + 8) * 128)) for jj0 in (0, 8)]

            win_done = cast_stream(win_src, ND, 2, win_out, [hti_done[-1]])

            def wo_src(k, h):
                return w_outT[k * 128:(k + 1) * 128, 0:D]

            def wo_out(k, h):
                return [(w_o_dram[:, :, k * 128:(k + 1) * 128],
                         (0, ND * 128))]

            wo_done = cast_stream(wo_src, NG, 1, wo_out, win_done)

            l_wx = P("sync", lambda e: e.dma_start(
                out=cast_sl(0)[:, 0:NG * 96].rearrange(
                    "p (g c) -> p g c", g=NG),
                in_=w_xT.rearrange("(g p) c -> p g c", p=128)),
                dma=True, dsem="cs0", after=wo_done)
            c_wx = P("vector", lambda e: e.tensor_copy(
                w_xT_bf[:], cast_sl(0)[:, 0:NG * 96]), after=[l_wx])
            l_wdt = P("sync", lambda e: e.dma_start(
                out=cast_sl(1)[0:64, :], in_=w_dtT[:]),
                dma=True, dsem="cs1", after=wo_done)
            c_wdt = P("vector", lambda e: e.tensor_copy(
                w_dtT_bf[:], cast_sl(1)[0:64, :]), after=[l_wdt, c_wx])

            phase0_done = ([m_done[-1], a_fix, z_carry, z_tails, consts_done,
                            c_wx, c_wdt, c_eps, c_half] + win_done + wo_done)

            ln_ctr = [0]

            def ln_stats(src_ap, after):
                c = ln_ctr[0] % 8
                ln_ctr[0] += 1
                col = lambda i: stats[:, 8 * c + i: 8 * c + i + 1]
                mraw, sraw, m_col, em_col, lv_col, r_col, mr_col, nmr_col = (
                    col(i) for i in range(8))
                i1 = P("vector", lambda e: e.reduce_sum(mraw, src_ap, AX.X),
                       after=after)
                i2 = P("scalar", lambda e: e.activation(sq_dump[:], src_ap,
                                                        AF.Square,
                                                        accum_out=sraw),
                       after=after)
                c0 = P("scalar", lambda e: e.activation(m_col, mraw,
                                                        AF.Identity,
                                                        scale=1.0 / D),
                       after=[i1])
                c1 = P("scalar", lambda e: e.activation(em_col, m_col,
                                                        AF.Square),
                       after=[c0])
                c2 = P("scalar", lambda e: e.activation(
                    em_col, em_col, AF.Identity, scale=-1.0,
                    bias=stats[:, 120:121]), after=[c1, c_eps])
                c3 = P("scalar", lambda e: e.activation(
                    lv_col, sraw, AF.Ln, scale=1.0 / D, bias=em_col),
                    after=[c2, i2])
                c4 = P("scalar", lambda e: e.activation(
                    r_col, lv_col, AF.Exp, scale=stats[:, 121:122]),
                    after=[c3, c_half])
                return m_col, r_col, [c4]

            prev_chunk_tail = [consts_done, c_eps, c_half, z_tails, z_carry,
                               a_fix]
            prev_apply = []
            store_ops = []
            prev_dA = {}
            prev_crep = []
            tr_evs = []
            xw_loads, xw_mms = [], []
            zw_loads, zw_mms = [], []
            wo_loads, wo_mms = [], []

            def transpose_block(src_fn, dst_fn, after_src, extra_first=()):
                evs = []
                for jj in range(ND):
                    pbt = [pb5, pb7][jj % 2]
                    sl = (jj // 2) * 128
                    war = tr_evs[-2:-1]
                    tr = P("tensor", lambda e, jj=jj, pbt=pbt, sl=sl, s=src_fn:
                           e.transpose(
                        pbt[:, sl: sl + 128], s(jj), id_b[:]),
                        after=list(after_src) + war +
                        (list(extra_first) if jj == 0 else []))
                    
                    ev = P("vector", lambda e, jj=jj, pbt=pbt, sl=sl, d=dst_fn:
                           e.tensor_copy(d(jj), pbt[:, sl: sl + 128]),
                           after=[tr])
                    
                    tr_evs.append(ev)
                    evs.append(ev)
                return evs

            for ch in range(nch):
                t0 = ch * TC
                par = ch % 2
                x_tile = x_tile2[:, par * D:(par + 1) * D]
                lnx_t = lnx_t2[:, par * D:(par + 1) * D]
                lnx_ready = []
                for q in range(NT):
                    lq = P("sync", lambda e, q=q, t0=t0, x_tile=x_tile:
                           e.dma_start(
                        out=x_tile, in_=xb[t0 + q * 128: t0 + (q + 1) * 128, :]),
                        dma=True, dsem="xt",
                        after=prev_apply + (prev_chunk_tail if q == 0 else []))
                    m_col, r_col, sdone = ln_stats(x_tile, [lq])
                    ap = P("vector", lambda e, m=m_col, r=r_col, lnx_t=lnx_t, x_tile=x_tile: e.tensor_scalar(
                        lnx_t, x_tile, m, r, ALU.subtract, ALU.mult),
                        after=sdone + tr_evs[-1:])
                    prev_apply = [ap]
                    if q == 0:
                        dump("lnt", lnx_t, [ap])
                    evs = transpose_block(
                        lambda jj, lnx_t=lnx_t: lnx_t[:, jj * 128:(jj + 1) * 128],
                        lambda jj, q=q: lnx_d[:, jj * TC + q * 128:
                                              jj * TC + q * 128 + 128],
                        [ap])
                    if q == 0 and ch == 0:
                        dump("tr0", lnx_d[:], [evs[0]])
                        dump("tr8", lnx_d[:], [evs[-1]])
                    lnx_ready += evs
                dump("lnx", lnx_d[:], [lnx_ready[-1]])
                if ch == 1 and dbg == "tails1":
                    dump("tails1", tails[:], [lnx_ready[-1]])

                conv_done = []
                silu_done = []
                for g in range(NG):
                    xidx = ch * NG + g
                    lx_w = P("sync", lambda e, g=g, xidx=xidx: e.dma_start(
                        out=xw_ring[:, (xidx % 3) * 1024: (xidx % 3) * 1024 + 1024],
                        in_=w_x_dram[g]), dma=True, dsem="xw%d" % (xidx % 3),
                        after=(xw_mms[xidx - 3: xidx - 2] if xidx >= 3
                               else win_done))
                    xw_loads.append(lx_w)
                    pz = [pb0, pb1][g % 2]

                    def fmm(e, g=g, pz=pz, xidx=xidx):
                        r = None
                        for k in range(ND):
                            r = e.matmul(pz[:, 0:TC],
                                         xw_ring[:, (xidx % 3) * 1024 + k * 128:
                                                 (xidx % 3) * 1024 + k * 128 + 128],
                                         lnx_d[:, k * TC:(k + 1) * TC],
                                         start=(k == 0), stop=(k == ND - 1))
                        return r
                    mm = P("tensor", fmm,
                           after=[lnx_ready[-1], lx_w] + conv_done[-2:-1] +
                           ([m_done[-1]] if ch == 0 and g == 0 else []))
                    xw_mms.append(mm)
                    evac = P("scalar", lambda e, pz=pz: e.copy(
                        xi_ext[:, 3:3 + TC], pz[:, 0:TC]),
                        after=[mm] + conv_done[-1:])
                    if ch == 0 and g == 0:
                        dump("xi0", xi_ext[:, 3:TC + 3], [evac])
                    if ch == 0 and g == 1:
                        dump("xi1", xi_ext[:, 3:TC + 3], [evac])
                    tl = tails[:, g * 4: g * 4 + 4]
                    k0 = P("vector", lambda e, tl=tl: e.tensor_copy(
                        xi_ext[:, 0:3], tl[:, 0:3]), after=conv_done[-1:])
                    k2 = P("vector", lambda e, tl=tl: e.tensor_copy(
                        tl[:, 0:3], xi_ext[:, TC:TC + 3]), after=[evac, k0])
                    acc = conv_acc[:, 0:TC]
                    cw = lambda j, g=g: convw_sb[:, g * 4 + j: g * 4 + j + 1]
                    c0 = P("vector", lambda e, acc=acc, cw=cw:
                           e.tensor_scalar_mul(acc, xi_ext[:, 3:3 + TC],
                                               cw(3)), after=[evac, k0])
                    c1 = P("vector", lambda e, acc=acc, cw=cw:
                           e.scalar_tensor_tensor(acc, xi_ext[:, 2:2 + TC],
                                                  cw(2), acc,
                                                  ALU.mult, ALU.add),
                           after=[c0])
                    c2 = P("vector", lambda e, acc=acc, cw=cw:
                           e.scalar_tensor_tensor(acc, xi_ext[:, 1:1 + TC],
                                                  cw(1), acc,
                                                  ALU.mult, ALU.add),
                           after=[c1])
                    c3 = P("vector", lambda e, acc=acc, cw=cw:
                           e.scalar_tensor_tensor(acc, xi_ext[:, 0:TC],
                                                  cw(0), acc,
                                                  ALU.mult, ALU.add),
                           after=[c2])
                    if ch == 0 and g == 0:
                        dump("cacc0", conv_acc[:, 0:TC], [c3])
                    sil = P("scalar", lambda e, g=g: e.activation(
                        xc_bf[:, g * TC:(g + 1) * TC], conv_acc[:, 0:TC],
                        AF.Silu, bias=convb_sb[:, g:g + 1]), after=[c3, k2])
                    conv_done.append(sil)
                dump("xc", xc_bf[:], [conv_done[-1]])
                dump("tails0", tails[:], [conv_done[-1]])

                def fxdbl(e):
                    r = None
                    for k in range(NG):
                        r = e.matmul(pb2[0:96, 0:TC],
                                     w_xT_bf[:, k * 96:(k + 1) * 96],
                                     xc_bf[:, k * TC:(k + 1) * TC],
                                     start=(k == 0), stop=(k == NG - 1))
                    return r
                xdbl = P("tensor", fxdbl, after=[conv_done[-1]] +
                         ([c_wx] if ch == 0 else []))
                ev_dtr = P("scalar", lambda e: e.copy(dtr_bf[:], pb2[0:64, 0:TC]),
                           after=[xdbl])
                ev_bc = P("scalar", lambda e: e.copy(bc_bf[:], pb2[64:96, 0:TC]),
                          after=[xdbl])
                d_out = P("sync", lambda e: e.dma_start(out=bc_stage[:],
                                                        in_=bc_bf[:]),
                          dma=True, dsem="bco", after=[ev_bc] + prev_crep +
                          ([c_wdt] if ch == 0 else []))
                d_brep_a = P("sync", lambda e: e.dma_start(
                    out=B_rep()[:, 0:8 * TC].rearrange(
                        "p (s t) -> p s t", s=8),
                    in_=bc_stage[0:8, :].unsqueeze(0).broadcast_to((128, 8, TC))),
                    dma=True, dsem="bcb", after=[d_out])
                d_brep = P("sync", lambda e: e.dma_start(
                    out=B_rep()[:, 8 * TC:16 * TC].rearrange(
                        "p (s t) -> p s t", s=8),
                    in_=bc_stage[8:16, :].unsqueeze(0).broadcast_to((128, 8, TC))),
                    dma=True, dsem="bcb1", after=[d_out])
                d_crep = P("sync", lambda e: e.dma_start(
                    out=C_rep().rearrange("p (s t) -> p s t", s=S),
                    in_=bc_stage[16:32, :].unsqueeze(0).broadcast_to((128, S, TC))),
                    dma=True, dsem="bcc", after=[d_out])
                prev_crep = [d_crep]

                dump("dtr", dtr_bf[:], [ev_dtr])
                dump("bc", bc_bf[:], [ev_bc])
                dump("brep", B_rep(), [d_brep])

                yprime_done = []
                prev_sil = []
                prev_soft = []
                prev_scan = []
                prev_yz = []
                prev_pr = []
                prev_a1 = []
                prev_a4 = []
                prev_uprep = []
                dtv_l = {}
                dtx_l = {}

                def dt_chain(g):
                    pdt = [pb3, pb4][g % 2]
                    dmm = P("tensor", lambda e, g=g, pdt=pdt: e.matmul(
                        pdt[:, 0:TC], w_dtT_bf[:, g * 128:(g + 1) * 128],
                        dtr_bf[:], start=True, stop=True),
                        after=[ev_dtr] + prev_soft[-2:-1] +
                        ([c_wdt] if ch == 0 and g == 0 else []))
                    sp1 = P("scalar", lambda e, g=g, pdt=pdt: e.activation(
                        scr_f[:, 0:TC], pdt[:, 0:TC], AF.Exp,
                        bias=bdt_sb[:, g:g + 1]),
                        after=[dmm])
                    dtv = dt_b[:, (g % 2) * TC:(g % 2) * TC + TC]
                    war_dt = [prev_scan[-1]] if g >= 2 else []
                    sp2 = P("scalar", lambda e, dtv=dtv: e.activation(
                        dtv, scr_f[:, 0:TC], AF.Ln, bias=1.0),
                        after=[sp1] + war_dt)
                    prev_soft.append(sp2)
                    dtx = P("gpsimd", lambda e, g=g, dtv=dtv: e.tensor_mul(
                        dtx_bf[:], dtv, xc_bf[:, g * TC:(g + 1) * TC]),
                        after=[sp2] + prev_uprep[-1:])
                    dtv_l[g] = dtv
                    dtx_l[g] = dtx

                dt_chain(0)
                if ch == 0 and dbg == "dt":
                    dump("dt", dt_b[:, 0:TC], [prev_soft[0]])
                for g in range(NG):
                    if g % 4 == 0:
                        zbanks = [pb0, pb1, pb2, pb6]
                        for gq in range(g, g + 4):
                            zidx = ch * NG + gq
                            lz_w = P("sync", lambda e, gq=gq, zidx=zidx:
                                     e.dma_start(
                                out=zw_ring[:, (zidx % 3) * 1024:
                                            (zidx % 3) * 1024 + 1024],
                                in_=w_z_dram[gq]), dma=True,
                                dsem="zw%d" % (zidx % 3),
                                after=(zw_mms[zidx - 3: zidx - 2] if zidx >= 3
                                       else win_done))
                            zw_loads.append(lz_w)
                            pzq = zbanks[gq % 4]

                            def fz(e, gq=gq, pzq=pzq, zidx=zidx):
                                r = None
                                for k in range(ND):
                                    r = e.matmul(
                                        pzq[:, 0:TC],
                                        zw_ring[:, (zidx % 3) * 1024 + k * 128:
                                                (zidx % 3) * 1024 + k * 128 + 128],
                                        lnx_d[:, k * TC:(k + 1) * TC],
                                        start=(k == 0), stop=(k == ND - 1))
                                return r
                            if g == 0:
                                zwar = [conv_done[-1]]
                                if gq % 4 == 2:
                                    zwar += [ev_dtr, ev_bc]
                            else:
                                zwar = prev_sil[-4:-3]
                            zmm = P("tensor", fz, after=zwar + [lz_w])
                            zw_mms.append(zmm)
                            szv = sz_b[:, (gq % 4) * TC:(gq % 4) * TC + TC]
                            war_sz = [prev_yz[-4]] if len(prev_yz) >= 4 else []
                            sil = P("scalar", lambda e, szv=szv, pzq=pzq:
                                    e.activation(szv, pzq[:, 0:TC], AF.Silu),
                                    after=[zmm] + war_sz)
                            prev_sil.append(sil)
                    szv = sz_b[:, (g % 4) * TC:(g % 4) * TC + TC]
                    dtv = dtv_l[g]
                    up_a = P("vector", lambda e: e.tensor_tensor(
                        u_v()[:, 0:8 * TC].rearrange("p (s t) -> p s t", s=8),
                        dtx_bf[:].unsqueeze(1).broadcast_to((128, 8, TC)),
                        B_rep()[:, 0:8 * TC].rearrange("p (s t) -> p s t", s=8),
                        ALU.mult), after=[dtx_l[g], d_brep_a] + prev_pr[-1:] +
                        ([m_done[-1]] if ch == 0 and g == 0 else []))
                    uprep = P("vector", lambda e: e.tensor_tensor(
                        u_v()[:, 8 * TC:16 * TC].rearrange(
                            "p (s t) -> p s t", s=8),
                        dtx_bf[:].unsqueeze(1).broadcast_to((128, 8, TC)),
                        B_rep()[:, 8 * TC:16 * TC].rearrange(
                            "p (s t) -> p s t", s=8),
                        ALU.mult), after=[dtx_l[g], d_brep] + prev_pr[-1:] +
                        ([m_done[-1]] if ch == 0 and g == 0 else []))
                    prev_uprep.append(uprep)
                    if g == 0:
                        dump("u", u_v()[:], [uprep])
                    scans = []
                    for s in range(S):
                        dav = da_b[:, (s % 4) * TC:(s % 4) * TC + TC]
                        war_da = [prev_dA[s % 4]] if (s % 4) in prev_dA else []
                        dexp = P("scalar", lambda e, g=g, s=s, dav=dav, dtv=dtv:
                                 e.activation(dav, dtv, AF.Exp,
                                              scale=a_sb[:, g * S + s:
                                                         g * S + s + 1]),
                                 after=[prev_soft[g]] + war_da)
                        sc = P("vector", lambda e, g=g, s=s, dav=dav:
                               e.tensor_tensor_scan(
                                   u_v()[:, s * TC:(s + 1) * TC], dav,
                                   u_v()[:, s * TC:(s + 1) * TC],
                                   carry[:, g * S + s: g * S + s + 1],
                                   ALU.mult, ALU.add),
                               after=[dexp, up_a if s < 8 else uprep])
                        prev_dA[s % 4] = sc
                        scans.append(sc)
                    prev_scan.append(scans[-1])
                    cu = P("vector", lambda e, g=g: e.tensor_copy(
                        carry[:, g * S:(g + 1) * S].unsqueeze(-1),
                        u_v().rearrange("p (s t) -> p s t", s=S)[:, :, TC - 1:TC]),
                        after=[scans[-1]])
                    if g == 0:
                        dump("scan", u_v()[:], [cu])
                    if g + 1 < NG:
                        dt_chain(g + 1)
                    pr = P("vector", lambda e: e.tensor_tensor(
                        prb()[:], u_v()[:], C_rep()[:], ALU.mult),
                        after=[cu, d_crep] + prev_a4[-1:] +
                        ([m_done[-1]] if ch == 0 and g == 0 else []))
                    prev_pr.append(pr)
                    a1 = P("vector", lambda e: e.tensor_add(
                        prb()[:, 0:8 * TC], prb()[:, 0:8 * TC],
                        prb()[:, 8 * TC:16 * TC]), after=[pr])
                    prev_a1.append(a1)
                    a2 = P("vector", lambda e: e.tensor_add(
                        prb()[:, 0:4 * TC], prb()[:, 0:4 * TC],
                        prb()[:, 4 * TC:8 * TC]), after=[a1])
                    a3 = P("vector", lambda e: e.tensor_add(
                        prb()[:, 0:2 * TC], prb()[:, 0:2 * TC],
                        prb()[:, 2 * TC:4 * TC]), after=[a2])
                    a4 = P("vector", lambda e: e.tensor_add(
                        y_f[:], prb()[:, 0:TC], prb()[:, TC:2 * TC]),
                        after=[a3] + prev_yz[-1:])
                    prev_a4.append(a4)
                    yz = P("vector", lambda e, szv=szv: e.tensor_mul(
                        yz_f[:], y_f[:], szv), after=[a4, prev_sil[g]])
                    prev_yz.append(yz)
                    yp2 = P("vector", lambda e, g=g: e.scalar_tensor_tensor(
                        xc_bf[:, g * TC:(g + 1) * TC],
                        xc_bf[:, g * TC:(g + 1) * TC],
                        d_sb[:, g:g + 1], yz_f[:], ALU.mult, ALU.add),
                        after=[yz, dtx_l[g]])
                    yprime_done.append(yp2)
                    if g == 0:
                        dump("y", y_f[:], [a4])
                        dump("yp", xc_bf[:, 0:TC], [yp2])

                wo_ev = []
                for m in range(ND):
                    widx = ch * ND + m
                    lw = P("sync", lambda e, m=m, widx=widx: e.dma_start(
                        out=wo_ring[:, (widx % 2) * 2048: (widx % 2) * 2048 + 2048],
                        in_=w_o_dram[m]), dma=True, dsem="wo%d" % (widx % 2),
                        after=(wo_mms[widx - 2: widx - 1] if widx >= 2
                               else wo_done))
                    wo_loads.append(lw)
                    pz = [pb0, pb1][m % 2]

                    def fwo(e, m=m, pz=pz, widx=widx):
                        r = None
                        for k in range(NG):
                            r = e.matmul(pz[:, 0:TC],
                                         wo_ring[:, (widx % 2) * 2048 + k * 128:
                                                 (widx % 2) * 2048 + k * 128 + 128],
                                         xc_bf[:, k * TC:(k + 1) * TC],
                                         start=(k == 0), stop=(k == NG - 1))
                        return r
                    mm = P("tensor", fwo,
                           after=[yprime_done[-1], prev_sil[-1], lw,
                                  prev_sil[-4], prev_sil[-3]] + wo_ev[-2:-1])
                    wo_mms.append(mm)
                    war_mo = [d_brep] if m == 0 else []
                    ev = P("scalar", lambda e, m=m, pz=pz: e.copy(
                        mo_d()[:, m * TC:(m + 1) * TC], pz[:, 0:TC]),
                        after=[mm] + war_mo + prev_pr[-1:])
                    wo_ev.append(ev)
                dump("mo", mo_d()[:], [wo_ev[-1]])

                mo_t_ev = []
                for q in range(NT):
                    evs = transpose_block(
                        lambda jj, q=q: mo_d()[:, jj * TC + q * 128:
                                               jj * TC + q * 128 + 128],
                        lambda jj, q=q: mo_t()[:, q * D + jj * 128:
                                               q * D + jj * 128 + 128],
                        [wo_ev[-1]],
                        extra_first=([d_crep, prev_pr[-1]] if q == 0 else []))
                    mo_t_ev += evs
                dump("mot", mo_t(), [mo_t_ev[-1]])

                ln3_ready = []
                prev_ln3ap = []
                apply7 = None
                for q in range(NT):
                    prev_stores_q = (store_ops[-4 + q:][:1]
                                     if len(store_ops) >= 4 else [])
                    movq = mo_t()[:, q * D:(q + 1) * D]
                    m2, r2, sdone = ln_stats(movq, [mo_t_ev[-1]])
                    lx = P("sync", lambda e, q=q, t0=t0, x_tile=x_tile:
                           e.dma_start(
                        out=x_tile, in_=xb[t0 + q * 128: t0 + (q + 1) * 128, :]),
                        dma=True, dsem="xt", after=prev_apply)
                    x2q = x2_v()[:, q * D:(q + 1) * D]
                    sv = scr_f[:, D:2 * D]
                    aa0 = P("vector", lambda e, movq=movq, m=m2, r=r2, sv=sv:
                            e.tensor_scalar(sv, movq, m, r,
                                            ALU.subtract, ALU.mult),
                            after=sdone + prev_apply)
                    if ch == 0 and q == 0:
                        dump("sv0", sv, [aa0])
                    aa = P("vector", lambda e, x2q=x2q, sv=sv, x_tile=x_tile: e.tensor_add(
                        x2q, sv, x_tile), after=[aa0, lx] + prev_stores_q)
                    if ch == 0 and q == 0:
                        dump("xq0", x_tile, [aa])
                        dump("x2q0", x2_v()[:, 0:D], [aa])
                    prev_apply = [aa]
                    apply7 = aa
                    m3, r3, s3done = ln_stats(x2q, [aa])
                    ap3 = P("vector", lambda e, x2q=x2q, m3=m3, r3=r3, lnx_t=lnx_t:
                            e.tensor_scalar(lnx_t, x2q, m3, r3,
                                            ALU.subtract, ALU.mult),
                            after=s3done + prev_ln3ap + tr_evs[-1:])
                    prev_ln3ap = [ap3]
                    if ch == 0 and q == 0:
                        dump("ln3t", lnx_t, [ap3])
                    evs = transpose_block(
                        lambda jj, lnx_t=lnx_t: lnx_t[:, jj * 128:(jj + 1) * 128],
                        lambda jj, q=q: lnx_d[:, jj * TC + q * 128:
                                              jj * TC + q * 128 + 128],
                        [ap3])
                    if ch == 0 and q == 0:
                        dump("ln3e", lnx_d[:], [evs[-1]])
                    ln3_ready += evs
                dump("x2", x2_v()[:], [prev_apply[0], prev_ln3ap[0]])
                dump("ln3", lnx_d[:], [ln3_ready[-1]])

                ffn_ev = []
                for m in range(ND):
                    pz = [pb0, pb1][m % 2]

                    def fff(e, m=m, pz=pz):
                        r = None
                        for k in range(ND):
                            r = e.matmul(pz[:, 0:TC],
                                         m_bf[:, k * D + m * 128:
                                              k * D + m * 128 + 128],
                                         lnx_d[:, k * TC:(k + 1) * TC],
                                         start=(k == 0), stop=(k == ND - 1))
                        return r
                    mm = P("tensor", fff, after=[ln3_ready[-1]] + ffn_ev[-2:-1])
                    ev = P("scalar", lambda e, m=m, pz=pz: e.activation(
                        mo_d()[:, m * TC:(m + 1) * TC], pz[:, 0:TC], AF.Identity,
                        bias=bias_sb[:, m:m + 1]),
                        after=[mm] + (mo_t_ev[-1:] if m == 0 else []))
                    ffn_ev.append(ev)
                dump("ffn", mo_d()[:], [ffn_ev[-1]])

                ffn_t_ev = []
                for q in range(NT):
                    evs = transpose_block(
                        lambda jj, q=q: mo_d()[:, jj * TC + q * 128:
                                               jj * TC + q * 128 + 128],
                        lambda jj, q=q: mo_t()[:, q * D + jj * 128:
                                               q * D + jj * 128 + 128],
                        [ffn_ev[-1]],
                        extra_first=([prev_ln3ap[0]] if q == 0 else []))
                    ffn_t_ev += evs
                dump("ffnt", mo_t(), [ffn_t_ev[-1]])

                chunk_tail = []
                for q in range(NT):
                    fq = mo_t()[:, q * D:(q + 1) * D]
                    x2q = x2_v()[:, q * D:(q + 1) * D]
                    m4, r4, s4done = ln_stats(fq, [ffn_t_ev[-1]])
                    sv = scr_f[:, D:2 * D]
                    f0 = P("vector", lambda e, fq=fq, m=m4, r=r4, sv=sv:
                           e.tensor_scalar(sv, fq, m, r,
                                           ALU.subtract, ALU.mult),
                           after=s4done + prev_apply)
                    fin = P("vector", lambda e, x2q=x2q, sv=sv: e.tensor_add(
                        x2q, sv, x2q), after=[f0])
                    prev_apply = [fin]
                    if q == NT - 1:
                        dump("fout", x2_v(), [fin])
                    st = P("sync", lambda e, q=q, x2q=x2q, t0=t0:
                           e.dma_start(
                        out=yb[t0 + q * 128: t0 + (q + 1) * 128, :], in_=x2q),
                        dma=True, dsem="st%d" % q, after=[fin])
                    store_ops.append(st)
                    chunk_tail.append(fin)
                prev_chunk_tail = chunk_tail + store_ops[-4:]

            P("sync", lambda e: e.nop(), after=store_ops[-4:])
        except _StopBuild:
            pass

        plan.finalize()

        @block.sync
        def _(eng):
            plan.emit_engine("sync", eng, sems, dsems)
            for n, c in sorted(plan.dsem_counts.items()):
                eng.wait_ge(dsems[n], c)
            for n, c in plan.final_counts.items():
                if c and n != "sync":
                    eng.wait_ge(sems[n], c)

        @block.gpsimd
        def _(eng):
            plan.emit_engine("gpsimd", eng, sems, dsems)

        @block.tensor
        def _(eng):
            plan.emit_engine("tensor", eng, sems, dsems)

        @block.vector
        def _(eng):
            plan.emit_engine("vector", eng, sems, dsems)

        @block.scalar
        def _(eng):
            plan.emit_engine("scalar", eng, sems, dsems)

    nc._dbg_plan = plan
    return nc


def _host_prep(inp):
    f32 = np.float32
    x = np.ascontiguousarray(np.asarray(inp["L0_attn_x"], f32))
    W_in = np.asarray(inp["L0_attn_W_in"], f32)
    conv_w = np.asarray(inp["L0_attn_conv_w"], f32).reshape(E, 4)
    conv_b = np.asarray(inp["L0_attn_conv_b"], f32)
    W_x = np.asarray(inp["L0_attn_W_x"], f32)
    W_dt = np.asarray(inp["L0_attn_W_dt"], f32)
    b_dt = np.asarray(inp["L0_attn_b_dt"], f32)
    A_log = np.asarray(inp["L0_attn_A_log"], f32)
    Dv = np.asarray(inp["L0_attn_D"], f32)
    W_out = np.asarray(inp["L0_attn_W_out"], f32)
    wr = np.asarray(inp["L0_ffn_w_freq_real"], f32)
    wi = np.asarray(inp["L0_ffn_w_freq_imag"], f32)
    bias = np.asarray(inp["L0_ffn_bias"], f32)

    k = np.arange(D)
    ang = ((np.outer(k, k) % D).astype(np.float64)) * (2.0 * np.pi / D)
    Fr = np.cos(ang)
    Fi = -np.sin(ang)
    Gr = np.cos(ang) / D
    Gi = np.sin(ang) / D

    pl = lambda v: np.ascontiguousarray(v.reshape(NG, 128).T)
    base = {
        "w_inT": np.ascontiguousarray(W_in.T),
        "w_xT": np.ascontiguousarray(W_x.T),
        "w_dtT": np.ascontiguousarray(W_dt.T),
        "w_outT": np.ascontiguousarray(W_out.T),
        "convw_l": np.ascontiguousarray(
            conv_w.reshape(NG, 128, 4).transpose(1, 0, 2).reshape(128, NG * 4)),
        "convb_l": pl(conv_b),
        "bdt_l": pl(b_dt),
        "d_l": pl(Dv),
        "alog_l": np.ascontiguousarray(
            A_log.reshape(NG, 128, S).transpose(1, 0, 2).reshape(128, NG * S)),
        "wfr": np.ascontiguousarray(wr),
        "wfi": np.ascontiguousarray(wi),
        "bias_l": np.ascontiguousarray(bias.reshape(ND, 128).T),
        "fr_c": Fr.astype(BF),
        "fi_c": Fi.astype(BF),
        "fineg_c": (-Fi).astype(BF),
        "gr_c": Gr.astype(BF),
        "gineg_c": (-Gi).astype(BF),
        "idb_c": np.eye(128).astype(BF),
    }
    return x, base


def kernel(**inp):
    x, base = _host_prep(inp)
    nc = build_nc()
    in_maps = [dict(base, xb=np.ascontiguousarray(x[b])) for b in range(8)]
    res = run_bass_kernel_spmd(nc, in_maps, core_ids=list(range(8)))
    out = np.stack([np.asarray(res.results[b]["yb"]) for b in range(8)], axis=0)
    return out.astype(np.float32)
